# version 13
# baseline (speedup 1.0000x reference)
import math
import contextlib
import numpy as np
import concourse.bass as bass
import concourse.mybir as mybir
from concourse.bass_utils import run_bass_kernel_spmd

F32 = mybir.dt.float32
BF16 = mybir.dt.bfloat16
I8 = mybir.dt.int8
AF = mybir.ActivationFunctionType
ALU = mybir.AluOpType
AX = mybir.AxisListType

S = 2048
D = 2048
H = 16
G = 4
DC = 16
NT = 4
IC = 64
SBW = 32
TS = 512
INV_SQRT_HD = 1.0 / math.sqrt(128.0)
NEG_CBOUND = -1.02 / (2.0 * math.sqrt(128.0))


class Sem:
    def __init__(self, h):
        self.h = h
        self.n = 0

    def inc(self, k=1):
        self.n += k
        return self.n


def build_program():
    nc = bass.Bass("TRN2", target_bir_lowering=False)
    dt = nc.dram_tensor
    x_nat = dt("x_nat", [S, D], F32, kind="ExternalInput").ap()
    xTd = dt("xT", [D, S], F32, kind="ExternalInput").ap()
    wqd = dt("wq", [H, 128, DC, 128], F32, kind="ExternalInput").ap()
    wkd = dt("wk", [G, 128, DC, 128], F32, kind="ExternalInput").ap()
    wvd = dt("wv", [128, DC, 512], F32, kind="ExternalInput").ap()
    wod = dt("wo", [DC, 128, DC, 128], F32, kind="ExternalInput").ap()
    wgd = dt("wg8", [SBW, 128, DC, 256], I8, kind="ExternalInput").ap()
    wud = dt("wu8", [SBW, 128, DC, 256], I8, kind="ExternalInput").ap()
    wdd = dt("wd8", [DC, 128, IC, 128], I8, kind="ExternalInput").ap()
    sgd = dt("sg", [128, IC], F32, kind="ExternalInput").ap()
    sud = dt("su", [128, IC], F32, kind="ExternalInput").ap()
    sdd = dt("sd", [128, DC], F32, kind="ExternalInput").ap()
    cosd = dt("cosT", [64, S], F32, kind="ExternalInput").ap()
    sind = dt("sinT", [64, S], F32, kind="ExternalInput").ap()
    outd = dt("outT", [D, S], F32, kind="ExternalOutput").ap()
    ascr = dt("attn_scr", [128, H, S], BF16).ap()
    xscr = dt("xn_scr", [128, DC, S], BF16).ap()

    streams = {"t": [], "v": [], "s": [], "g": [], "y": []}
    waited = {k: {} for k in streams}

    def em(eng, fn):
        streams[eng].append(fn)

    def ew(eng, sem, val):
        if val > 0 and waited[eng].get(id(sem.h), 0) < val:
            waited[eng][id(sem.h)] = val
            em(eng, lambda e, h=sem.h, v=val: e.wait_ge(h, v))

    stack = contextlib.ExitStack()
    with stack:
        mksem = lambda name: Sem(stack.enter_context(nc.semaphore(name)))
        st_ = mksem("s_t")
        sv = mksem("s_v")
        ss = mksem("s_s")
        sgc = mksem("s_gc")
        stq = [mksem("s_tq%d" % i) for i in range(NT)]
        L = {}
        for name in ("cs", "x0", "x1", "xscr", "wk", "wv", "wq0",
                     "wq1", "scl", "ao0", "ao1", "att", "xnt0", "xnt1",
                     "xt0", "xt1", "out0", "out1", "wg0", "wg1", "wu0",
                     "wu1", "wd0", "wd1", "wo0", "wo1"):
            L[name] = mksem("l_" + name)

        def dma(eng, lane, build):
            em(eng, lambda e, b=build, h=L[lane].h: b(e).then_inc(h, 16))
            return L[lane].inc(16)

        def ev(build, chain=True):
            prev = sv.n
            if chain:
                em("v", lambda v, b=build, p=prev: (
                    v.wait_ge(sv.h, p) if p else None,
                    b(v).then_inc(sv.h, 1)))
            else:
                em("v", lambda v, b=build: b(v).then_inc(sv.h, 1))
            return sv.inc(1)

        def es(build, chain=True):
            prev = ss.n
            if chain:
                em("s", lambda s, b=build, p=prev: (
                    s.wait_ge(ss.h, p) if p else None,
                    b(s).then_inc(ss.h, 1)))
            else:
                em("s", lambda s, b=build: b(s).then_inc(ss.h, 1))
            return ss.inc(1)

        ps_sc = stack.enter_context(nc.psum_tensor("ps_sc", [128, 4, 512], F32))
        ps_av = [stack.enter_context(nc.psum_tensor("ps_av%d" % i, [128, 512], F32))
                 for i in range(2)]
        ps_sm = [stack.enter_context(nc.psum_tensor("ps_sm%d" % i, [128, 512], F32))
                 for i in range(2)]

        def PS(b):
            if b < 4:
                return ps_sc[:, b, :]
            if b < 6:
                return ps_av[b - 4][:]
            return ps_sm[b - 6][:]

        psum_free = [None] * 8

        def te_claim(bank):
            c = psum_free[bank]
            if c is not None:
                ew("t", c[0], c[1])
                psum_free[bank] = None

        def mm_group(n, build_i):
            for i in range(n):
                if i < n - 1:
                    em("t", lambda t_, ii=i: build_i(t_, ii))
                else:
                    em("t", lambda t_, ii=i: build_i(t_, ii).then_inc(st_.h, 1))
            return st_.inc(1)

        sb_ = lambda name, shape, dtype: stack.enter_context(
            nc.sbuf_tensor(name, shape, dtype))

        ones_bf = sb_("ones_bf", [128, 128], BF16)
        ones1f = sb_("ones1f", [1, 128], F32)
        sg_sb = sb_("sg_sb", [128, IC], F32)
        su_sb = sb_("su_sb", [128, IC], F32)
        sd_sb = sb_("sd_sb", [128, DC], F32)
        stats = sb_("stats", [128, 2, 8], F32)
        smal = sb_("smal", [1, 32], F32)
        bias_h = sb_("bias_h", [128, 1], F32)
        rb_sb = sb_("rb_sb", [128, 512], F32)
        eps_sb = sb_("eps_sb", [128, 1], F32)
        isq_sb = sb_("isq_sb", [128, 1], F32)
        clo_sb = sb_("clo_sb", [128, 2], F32)

        import numpy as _np
        _eps_hi = float(_np.float32(1e-5))
        _eps_lo = float(_np.float32(1e-5 - _np.float64(_np.float32(1e-5))))
        _isq_hi = float(_np.float32(INV_SQRT_HD))
        _isq_lo = float(_np.float32(INV_SQRT_HD
                                    - _np.float64(_np.float32(INV_SQRT_HD))))
        em("g", lambda g: g.memset(ones_bf[:], 1.0))
        em("g", lambda g: g.memset(ones1f[:], 1.0))
        em("g", lambda g: g.memset(eps_sb[:], _eps_hi))
        em("g", lambda g: g.memset(isq_sb[:], _isq_hi))
        em("g", lambda g: g.memset(clo_sb[:, 0:1], _eps_lo))
        em("g", lambda g: g.memset(clo_sb[:, 1:2], _isq_lo).then_inc(sgc.h, 1))
        memset_done = sgc.inc(1)
        ew("v", sgc, memset_done)
        ev(lambda v: v.tensor_add(eps_sb[:], eps_sb[:], clo_sb[:, 0:1]))
        consts_v = ev(lambda v: v.tensor_add(
            isq_sb[:], isq_sb[:], clo_sb[:, 1:2]))
        ew("s", sv, consts_v)
        for dst, src in ((sg_sb, sgd), (su_sb, sud), (sd_sb, sdd)):
            dma("g", "scl", lambda g, d=dst, s=src: g.dma_start(d[:], s))
        scales_done = L["scl"].n

        with contextlib.ExitStack() as sa:
            sba = lambda name, shape, dtype: sa.enter_context(
                nc.sbuf_tensor(name, shape, dtype))
            xn = sba("xn", [128, DC, S], BF16)
            cos_sb = sba("cos_sb", [128, S], F32)
            sin_sb = sba("sin_sb", [128, S], F32)
            kr = sba("kr", [128, G, S], BF16)
            vtm = sba("vtm", [128, DC, 512], BF16)

            dma("y", "cs", lambda y: y.dma_start(cos_sb[0:64, :], cosd))
            dma("y", "cs", lambda y: y.dma_start(cos_sb[64:128, :], cosd))
            dma("y", "cs", lambda y: y.dma_start(sin_sb[0:64, :], sind))
            dma("y", "cs", lambda y: y.dma_start(sin_sb[64:128, :], sind))
            cos_done = L["cs"].n

            with contextlib.ExitStack() as s0:
                sb0 = lambda name, shape, dtype: s0.enter_context(
                    nc.sbuf_tensor(name, shape, dtype))
                x_t = [sb0("x_t%d" % i, [128, D], F32) for i in range(2)]
                xn_tm = sb0("xn_tm", [128, 16, D], BF16)
                scr = sb0("scr", [128, D], F32)

                applyv = [0] * 16
                for i in range(16):
                    par = i % 2
                    lane = "x%d" % par
                    if i == 0:
                        ew("s", sgc, memset_done)
                    if i >= 2:
                        ew("g", ss, applyv[i - 2])
                    xload = dma("g", lane, lambda g, p=par, i0=i * 128:
                                g.dma_start(x_t[p][:], x_nat[i0:i0 + 128, :]))
                    ew("s", L[lane], xload)
                    stats_done = es(lambda s, p=par: s.activation(
                        scr[:], x_t[p][:], AF.Square, accum_out=stats[:, p, 1:2]))
                    ew("v", L[lane], xload)
                    ev(lambda v, p=par: v.reduce_sum(
                        stats[:, p, 0:1], x_t[p][:], axis=AX.X))
                    ew("v", ss, stats_done)
                    ev(lambda v, p=par: v.tensor_scalar_mul(
                        stats[:, p, 2:3], stats[:, p, 0:1], 1.0 / D))
                    ev(lambda v, p=par: v.tensor_scalar_mul(
                        stats[:, p, 3:4], stats[:, p, 1:2], 1.0 / D))
                    ev(lambda v, p=par: v.tensor_mul(
                        stats[:, p, 4:5], stats[:, p, 2:3], stats[:, p, 2:3]))
                    varv = ev(lambda v, p=par: v.tensor_sub(
                        stats[:, p, 5:6], stats[:, p, 3:4], stats[:, p, 4:5]))
                    ew("s", sv, varv)
                    sqv = es(lambda s, p=par: s.activation(
                        stats[:, p, 6:7], stats[:, p, 5:6], AF.Sqrt,
                        bias=eps_sb[:]))
                    ew("v", ss, sqv)
                    ev(lambda v, p=par: v.reciprocal(
                        stats[:, p, 7:8], stats[:, p, 6:7]))
                    ev(lambda v, p=par: v.tensor_mul(
                        stats[:, p, 4:5], stats[:, p, 2:3], stats[:, p, 7:8]))
                    nbv = ev(lambda v, p=par: v.tensor_scalar_mul(
                        stats[:, p, 5:6], stats[:, p, 4:5], -1.0))
                    ew("s", sv, nbv)
                    applyv[i] = es(lambda s, p=par, ii=i: s.activation(
                        xn_tm[:, ii, :], x_t[p][:], AF.Identity,
                        bias=stats[:, p, 5:6], scale=stats[:, p, 7:8]))
                    ew("y", ss, applyv[i])
                    for hh in range(2):
                        em("y", lambda y, ii=i, h2=hh, qt=i // 4:
                           y.dma_start_transpose(
                               xn[:, 8 * h2:8 * h2 + 8,
                                  ii * 128:ii * 128 + 128],
                               xn_tm[:, ii, 1024 * h2:1024 * h2 + 1024])
                           .then_inc(stq[qt].h, 16))
                        stq[i // 4].inc(16)
                qdone = [stq[i].n for i in range(NT)]
                for i in range(NT):
                    ew("y", stq[i], qdone[i])
                xscr_done = dma("y", "xscr",
                                lambda y: y.dma_start(xscr, xn[:]))

            wq_sb = [sba("wq_sb%d" % i, [128, DC, 128], BF16) for i in range(2)]
            wk_sb = sba("wk_sb", [128, G, DC, 128], BF16)
            wv_sb = sba("wv_sb", [128, DC, 512], BF16)
            ew("g", ss, applyv[15])
            for i in range(NT):
                ew("g", stq[i], qdone[i])
            for gk in range(G):
                dma("g", "wk", lambda g, i=gk: g.dma_start(
                    wk_sb[:, i, :, :], wkd[i]))
            wk_done = L["wk"].n
            wv_done = dma("g", "wv", lambda g: g.dma_start(wv_sb[:], wvd))
            wq_val = [None] * H
            wq_val[0] = dma("g", "wq0",
                            lambda g: g.dma_start(wq_sb[0][:], wqd[0]))
            tmp0 = sba("tmp0", [64, 512], F32)
            tmp1 = sba("tmp1", [64, 512], F32)
            expT = [sba("expT%d" % i, [128, DC, 512], BF16) for i in range(2)]
            qr = sba("qr", [128, S], BF16)
            qps = sba("qps", [128, 4, 512], BF16)
            sqt = [sba("sqt%d" % i, [128, 512], BF16) for i in range(2)]
            attn_o = [sba("attn_o%d" % i, [128, 512], BF16) for i in range(2)]

            def rope(psum, dst_e, dst_o, t0):
                ev(lambda v, p=psum, a=t0: v.tensor_mul(
                    tmp0[:], p[0:64, :], cos_sb[0:64, a:a + TS]))
                ev(lambda v, p=psum, a=t0: v.tensor_mul(
                    tmp1[:], p[64:128, :], sin_sb[0:64, a:a + TS]))
                ev(lambda v, d=dst_e: v.tensor_sub(d, tmp0[:], tmp1[:]))
                ev(lambda v, p=psum, a=t0: v.tensor_mul(
                    tmp0[:], p[0:64, :], sin_sb[0:64, a:a + TS]))
                ev(lambda v, p=psum, a=t0: v.tensor_mul(
                    tmp1[:], p[64:128, :], cos_sb[0:64, a:a + TS]))
                return ev(lambda v, d=dst_o: v.tensor_add(d, tmp0[:], tmp1[:]))

            def rope_sb(src, dst_e, dst_o, t0):
                ev(lambda v, p=src, a=t0: v.tensor_mul(
                    tmp0[:], p[0:64, :], cos_sb[0:64, a:a + TS]))
                ev(lambda v, p=src, a=t0: v.tensor_mul(
                    tmp1[:], p[64:128, :], sin_sb[64:128, a:a + TS]))
                ev(lambda v, d=dst_e: v.tensor_sub(d, tmp0[:], tmp1[:]))
                ev(lambda v, p=src, a=t0: v.tensor_mul(
                    tmp0[:], p[0:64, :], sin_sb[0:64, a:a + TS]))
                ev(lambda v, p=src, a=t0: v.tensor_mul(
                    tmp1[:], p[64:128, :], cos_sb[64:128, a:a + TS]))
                return ev(lambda v, d=dst_o: v.tensor_add(d, tmp0[:], tmp1[:]))

            def norm_max(src_ap, acc_slot, first, wait_t):
                par = norm_max.par
                norm_max.par ^= 1
                ew("s", st_, wait_t)
                if norm_max.free[par]:
                    ew("s", st_, norm_max.free[par])
                sq_done = es(lambda s, p=par, a=src_ap: s.activation(
                    sqt[p][:], a, AF.Square), chain=False)
                te_claim(6)
                ew("t", ss, sq_done)
                nm = mm_group(1, lambda t_, i, p=par: t_.matmul(
                    ps_sm[0][0:1, :], ones_bf[:, 0:1], sqt[p][:],
                    start=True, stop=True))
                norm_max.free[par] = nm
                ew("v", st_, nm)
                if first:
                    rmx = ev(lambda v, sl=acc_slot: v.reduce_max(
                        smal[0:1, sl:sl + 1], ps_sm[0][0:1, :], axis=AX.X))
                else:
                    ev(lambda v: v.reduce_max(
                        smal[0:1, 26:27], ps_sm[0][0:1, :], axis=AX.X))
                    rmx = ev(lambda v, sl=acc_slot: v.tensor_max(
                        smal[0:1, sl:sl + 1], smal[0:1, sl:sl + 1],
                        smal[0:1, 26:27]))
                psum_free[6] = (sv, rmx)
                return sq_done
            norm_max.par = 0
            norm_max.free = [0, 0]

            ew("t", sgc, memset_done)
            ew("v", L["cs"], cos_done)
            ew("t", L["wk"], wk_done)
            ew("t", L["wv"], wv_done)
            krope = [[0] * NT for _ in range(G)]
            kproj_last = 0
            for t in range(NT):
                ew("t", stq[t], qdone[t])
                for gk in range(G):
                    bank = (t * G + gk) % 4
                    te_claim(bank)
                    kp = mm_group(DC, lambda t_, d, b=bank, gg=gk, a=t * TS:
                                  t_.matmul(PS(b), wk_sb[:, gg, d, :],
                                            xn[:, d, a:a + TS],
                                            start=(d == 0), stop=(d == DC - 1)))
                    kproj_last = kp
                    sqd = norm_max(PS(bank), gk, t == 0, kp)
                    ew("v", st_, kp)
                    ew("v", ss, sqd)
                    krope[gk][t] = rope(
                        PS(bank), kr[0:64, gk, t * TS:t * TS + TS],
                        kr[64:128, gk, t * TS:t * TS + TS], t * TS)
                    psum_free[bank] = (sv, krope[gk][t])
                for tc in range(t * 4, t * 4 + 4):
                    bank = 4 + tc % 2
                    te_claim(bank)
                    vp = mm_group(DC, lambda t_, d, b=bank, c0=tc * 128:
                                  t_.matmul(PS(b), xn[:, d, c0:c0 + 128],
                                            wv_sb[:, d, :],
                                            start=(d == 0), stop=(d == DC - 1)))
                    ew("v", st_, vp)
                    vev = ev(lambda v, b=bank, c=tc: v.tensor_copy(
                        vtm[:, c, :], PS(b)), chain=False)
                    psum_free[bank] = (sv, vev)

            qproj_done = [0] * H
            expT_free = [0, 0]
            attn_o_free = [0, 0]
            exp_last = 0
            pend = None
            av_done = {}
            sums_done = {}

            def emit_avsums_batch(p, b):
                gkp, qq, pq = p
                if b == 0:
                    te_claim(4 + pq)
                    te_claim(6 + pq)
                    ew("t", ss, exp_last_of[(gkp, qq, pq)])
                for j in range(4):
                    kc = 4 * b + j
                    last = kc == DC - 1
                    em("t", lambda t_, k=kc, gg=gkp, p2=pq, la=last:
                       (t_.matmul(ps_av[p2][:], vtm[:, k, gg * 128:gg * 128 + 128],
                                  expT[p2][:, k, :], start=(k == 0),
                                  stop=(k == DC - 1), skip_group_check=True)
                        .then_inc(st_.h, 1)) if la else
                       t_.matmul(ps_av[p2][:], vtm[:, k, gg * 128:gg * 128 + 128],
                                 expT[p2][:, k, :], start=(k == 0),
                                 stop=(k == DC - 1), skip_group_check=True))
                    if last:
                        av_done[p] = st_.inc(1)
                for j in range(4):
                    kc = 4 * b + j
                    last = kc == DC - 1
                    em("t", lambda t_, k=kc, p2=pq, la=last:
                       (t_.matmul(ps_sm[p2][:], ones_bf[:], expT[p2][:, k, :],
                                  start=(k == 0), stop=(k == DC - 1),
                                  skip_group_check=True).then_inc(st_.h, 1))
                       if la else
                       t_.matmul(ps_sm[p2][:], ones_bf[:], expT[p2][:, k, :],
                                 start=(k == 0), stop=(k == DC - 1),
                                 skip_group_check=True))
                    if last:
                        sums_done[p] = st_.inc(1)

            def emit_attn_epilogue(p, hh, qq):
                gkp, q2, pq = p
                ew("v", st_, sums_done[p])
                if attn_o_free[pq]:
                    ew("v", L["ao%d" % pq], attn_o_free[pq])
                ev(lambda v, p2=pq: v.reciprocal(rb_sb[:], ps_sm[p2][:]))
                ao = ev(lambda v, p2=pq: v.tensor_mul(
                    attn_o[p2][:], ps_av[p2][:], rb_sb[:]))
                psum_free[4 + pq] = (sv, ao)
                psum_free[6 + pq] = (sv, ao)
                ew("y", sv, ao)
                attn_o_free[pq] = dma(
                    "y", "ao%d" % pq, lambda y, h3=hh, p2=pq, a=qq * TS:
                    y.dma_start(ascr[:, h3, a:a + TS], attn_o[p2][:]))
                expT_free[pq] = sums_done[p]

            exp_last_of = {}
            pend_head = [None, None]
            for h in range(H):
                gk = h % 4
                if h + 1 < H:
                    if h >= 1:
                        ew("g", st_, qproj_done[h - 1])
                    wq_val[h + 1] = dma(
                        "g", "wq%d" % ((h + 1) % 2), lambda g, i=h + 1,
                        p=(h + 1) % 2: g.dma_start(wq_sb[p][:], wqd[i]))
                ew("t", L["wq%d" % (h % 2)], wq_val[h])
                qrope = [0] * NT
                for t in range(NT):
                    te_claim(t)
                    qp = mm_group(DC, lambda t_, d, tt=t, hh=h:
                                  t_.matmul(ps_sc[:, tt, :],
                                            wq_sb[hh % 2][:, d, :],
                                            xn[:, d, tt * TS:tt * TS + TS],
                                            start=(d == 0), stop=(d == DC - 1)))
                    qproj_done[h] = qp
                    norm_max(ps_sc[:, t, :], 16, t == 0, qp)
                    evv = es(lambda s, tt=t: s.copy(
                        qps[:, tt, :], ps_sc[:, tt, :]), chain=False)
                    psum_free[t] = (ss, evv)
                qps_done = ss.n
                ev(lambda v, gg=gk: v.tensor_add(
                    smal[0:1, 24:25], smal[0:1, 16:17], smal[0:1, gg:gg + 1]))
                negc = ev(lambda v: v.tensor_scalar_mul(
                    smal[0:1, 25:26], smal[0:1, 24:25], NEG_CBOUND))
                te_claim(6)
                ew("t", sv, negc)
                bmm = mm_group(1, lambda t_, i: t_.matmul(
                    ps_sm[0][:, 0:1], ones1f[:], smal[0:1, 25:26],
                    start=True, stop=True))
                if exp_last:
                    ew("v", ss, exp_last)
                ew("v", st_, bmm)
                bias_done = ev(lambda v: v.tensor_copy(
                    bias_h[:], ps_sm[0][:, 0:1]))
                psum_free[6] = (sv, bias_done)
                ew("v", ss, qps_done)
                for t in range(NT):
                    qrope[t] = rope_sb(qps[:, t, :],
                                       qr[0:64, t * TS:t * TS + TS],
                                       qr[64:128, t * TS:t * TS + TS], t * TS)
                for q in range(NT):
                    qp = q % 2
                    exp_b = [0] * 4
                    for b in range(4):
                        if pend is not None:
                            emit_avsums_batch(pend, b)
                        if b == 0:
                            ew("t", sv, qrope[q])
                        for j in range(4):
                            te_claim(j)
                        for j in range(4):
                            kc = 4 * b + j
                            last = j == 3
                            em("t", lambda t_, jj=j, k=kc, gg=gk, a=q * TS,
                               la=last:
                               (t_.matmul(ps_sc[:, jj, :],
                                          kr[:, gg, k * 128:k * 128 + 128],
                                          qr[:, a:a + TS], start=True,
                                          stop=True, skip_group_check=True)
                                .then_inc(st_.h, 1)) if la else
                               t_.matmul(ps_sc[:, jj, :],
                                         kr[:, gg, k * 128:k * 128 + 128],
                                         qr[:, a:a + TS], start=True,
                                         stop=True, skip_group_check=True))
                        sc_done = st_.inc(1)
                        if q == 0 and b == 0:
                            ew("s", sv, bias_done)
                        if b == 0 and expT_free[qp]:
                            ew("s", st_, expT_free[qp])
                        ew("s", st_, sc_done)
                        exp_b[b] = es(lambda s, p=qp, b0=4 * b: s.activation(
                            expT[p][:, b0:b0 + 4, :], ps_sc[:, :, :], AF.Exp,
                            bias=bias_h[:], scale=isq_sb[:]), chain=False)
                        for j in range(4):
                            psum_free[j] = (ss, exp_b[b])
                    exp_last = exp_b[3]
                    exp_last_of[(gk, q, qp)] = exp_last
                    if pend is not None:
                        emit_attn_epilogue(pend, pend_head[0], pend_head[1])
                    pend = (gk, q, qp)
                    pend_head = [h, q]
            for b in range(4):
                emit_avsums_batch(pend, b)
            emit_attn_epilogue(pend, pend_head[0], pend_head[1])
            ao_tot = [L["ao0"].n, L["ao1"].n]
            pA_t, pA_v, pA_s = st_.n, sv.n, ss.n

        sbb = lambda name, shape, dtype: stack.enter_context(
            nc.sbuf_tensor(name, shape, dtype))
        fuse = sbb("fuse", [128, IC, TS], BF16)
        xn_t = [sbb("xn_t%d" % i, [128, DC, TS], BF16) for i in range(2)]
        wg_sb = [sbb("wg_sb%d" % i, [128, DC, 256], BF16) for i in range(2)]
        wu_sb = [sbb("wu_sb%d" % i, [128, DC, 256], BF16) for i in range(2)]
        wd_sb = [sbb("wd_sb%d" % i, [128, IC, 128], BF16) for i in range(2)]
        wo_sb = [sbb("wo_sb%d" % i, [128, DC, 128], BF16) for i in range(2)]
        attn_t = sbb("attn_t", [128, H, TS], BF16)
        gate_sb = [sbb("gate_sb%d" % i, [128, TS], BF16) for i in range(2)]
        xT_sb = [sbb("xT_sb%d" % i, [128, TS], F32) for i in range(2)]
        out_sb = [sbb("out_sb%d" % i, [128, TS], F32) for i in range(2)]

        ew("g", st_, pA_t)
        ew("g", sv, pA_v)
        ew("g", ss, pA_s)
        ew("g", L["ao0"], ao_tot[0])
        ew("g", L["ao1"], ao_tot[1])
        ew("y", st_, pA_t)
        ew("y", sv, pA_v)
        ew("y", L["ao0"], ao_tot[0])
        ew("y", L["ao1"], ao_tot[1])

        wg_val = {}
        wu_val = {}
        wgu_buf_free = [0, 0]
        wd_val = {}
        wo_val = {}
        wd_buf_free = [0, 0]
        wo_buf_free = [0, 0]
        xn_t_val = [0] * NT
        xn_t_free = [0, 0]
        attn_t_val = [0] * NT
        gate_free = [0, 0]
        fuse_ward = 0
        out_free = [0, 0]
        xT_free = [0, 0]

        def load_wgu(t, sb):
            if wgu_buf_free[sb % 2]:
                ew("g", st_, wgu_buf_free[sb % 2])
                wgu_buf_free[sb % 2] = 0
            wg_val[(t, sb)] = dma(
                "g", "wg%d" % (sb % 2), lambda g, p=sb % 2, s=sb:
                g.dma_start(wg_sb[p][:], wgd[s]))
            wu_val[(t, sb)] = dma(
                "g", "wu%d" % (sb % 2), lambda g, p=sb % 2, s=sb:
                g.dma_start(wu_sb[p][:], wud[s]))

        def load_wdo(t, dc):
            if wd_buf_free[dc % 2]:
                ew("g", st_, wd_buf_free[dc % 2])
                wd_buf_free[dc % 2] = 0
            wd_val[(t, dc)] = dma(
                "g", "wd%d" % (dc % 2), lambda g, p=dc % 2, d=dc:
                g.dma_start(wd_sb[p][:], wdd[d]))
            if wo_buf_free[dc % 2]:
                ew("g", st_, wo_buf_free[dc % 2])
                wo_buf_free[dc % 2] = 0
            wo_val[(t, dc)] = dma(
                "g", "wo%d" % (dc % 2), lambda g, p=dc % 2, d=dc:
                g.dma_start(wo_sb[p][:], wod[d]))

        ew("y", L["xscr"], xscr_done)
        xn_t_val[0] = dma("y", "xnt0", lambda y: y.dma_start(
            xn_t[0][:], xscr[:, :, 0:TS]))
        attn_t_val[0] = dma("y", "att", lambda y: y.dma_start(
            attn_t[:], ascr[:, :, 0:TS]))
        load_wgu(0, 0)

        for t in range(NT):
            t0 = t * TS
            up_done = [0] * IC
            fuse_done = [0] * IC
            down_done = [0] * DC
            wo_done = [0] * DC
            for ic in range(IC):
                sb = ic // 2
                jc = (ic % 2) * 128
                if ic % 2 == 0:
                    if sb + 1 < SBW:
                        load_wgu(t, sb + 1)
                    elif t + 1 < NT:
                        load_wgu(t + 1, 0)
                if ic == 4:
                    load_wdo(t, 0)
                if ic == 8:
                    load_wdo(t, 1)
                bank = ic % 2
                te_claim(bank)
                ew("t", L["wg%d" % (sb % 2)], wg_val[(t, sb)])
                if ic == 0:
                    ew("t", L["xnt%d" % (t % 2)], xn_t_val[t])
                gate_done = mm_group(
                    DC, lambda t_, d, b=bank, p=sb % 2, j=jc, tp=t % 2:
                    t_.matmul(PS(b), wg_sb[p][:, d, j:j + 128],
                              xn_t[tp][:, d, :],
                              start=(d == 0), stop=(d == DC - 1)))
                ew("s", st_, gate_done)
                if t == 0 and ic == 0:
                    ew("s", L["scl"], scales_done)
                if gate_free[ic % 2]:
                    ew("s", sv, gate_free[ic % 2])
                silu_v = es(lambda s, b=bank, p=ic % 2, i=ic: s.activation(
                    gate_sb[p][:], PS(b), AF.Silu, scale=sg_sb[:, i:i + 1]),
                    chain=False)
                psum_free[bank] = (ss, silu_v)
                bank2 = 2 + ic % 2
                te_claim(bank2)
                ew("t", L["wu%d" % (sb % 2)], wu_val[(t, sb)])
                up_done[ic] = mm_group(
                    DC, lambda t_, d, b=bank2, p=sb % 2, j=jc, tp=t % 2:
                    t_.matmul(PS(b), wu_sb[p][:, d, j:j + 128],
                              xn_t[tp][:, d, :],
                              start=(d == 0), stop=(d == DC - 1)))
                wgu_buf_free[sb % 2] = up_done[ic]
                ew("v", st_, up_done[ic])
                ew("v", ss, silu_v)
                if ic == 0 and fuse_ward:
                    ew("v", st_, fuse_ward)
                fuse_done[ic] = ev(
                    lambda v, b=bank2, p=ic % 2, i=ic: v.scalar_tensor_tensor(
                        fuse[:, i, :], PS(b), su_sb[:, i:i + 1], gate_sb[p][:],
                        ALU.mult, ALU.mult), chain=False)
                gate_free[ic % 2] = fuse_done[ic]
                psum_free[bank2] = (sv, fuse_done[ic])
            if t + 1 < NT:
                if xn_t_free[(t + 1) % 2]:
                    ew("y", st_, xn_t_free[(t + 1) % 2])
                xn_t_val[t + 1] = dma(
                    "y", "xnt%d" % ((t + 1) % 2),
                    lambda y, a=(t + 1) * TS, p=(t + 1) % 2:
                    y.dma_start(xn_t[p][:], xscr[:, :, a:a + TS]))
            for dc in range(DC):
                bank = 4 + dc % 2
                te_claim(bank)
                ew("t", L["wd%d" % (dc % 2)], wd_val[(t, dc)])
                if dc == 0:
                    ew("t", sv, fuse_done[IC - 1])
                down_done[dc] = mm_group(
                    IC, lambda t_, i, b=bank, p=dc % 2:
                    t_.matmul(PS(b), wd_sb[p][:, i, :], fuse[:, i, :],
                              start=(i == 0), stop=(i == IC - 1)))
                wd_buf_free[dc % 2] = down_done[dc]
                bank2 = 6 + dc % 2
                te_claim(bank2)
                ew("t", L["wo%d" % (dc % 2)], wo_val[(t, dc)])
                if dc == 0:
                    ew("t", L["att"], attn_t_val[t])
                wo_done[dc] = mm_group(
                    DC, lambda t_, f, b=bank2, p=dc % 2:
                    t_.matmul(PS(b), wo_sb[p][:, f, :], attn_t[:, f, :],
                              start=(f == 0), stop=(f == DC - 1)))
                wo_buf_free[dc % 2] = wo_done[dc]
                if 1 <= dc < DC - 1:
                    load_wdo(t, dc + 1)
                xtlane = "xt%d" % (dc % 2)
                if xT_free[dc % 2]:
                    ew("y", sv, xT_free[dc % 2])
                xT_val = dma("y", xtlane, lambda y, p=dc % 2, r0=dc * 128, a=t0:
                             y.dma_start(xT_sb[p][:], xTd[r0:r0 + 128, a:a + TS]))
                ew("v", st_, wo_done[dc])
                ew("v", L[xtlane], xT_val)
                if out_free[dc % 2]:
                    ew("v", L["out%d" % (dc % 2)], out_free[dc % 2])
                ev(lambda v, b=bank, p=dc % 2, d=dc: v.scalar_tensor_tensor(
                    out_sb[p][:], PS(b), sd_sb[:, d:d + 1], xT_sb[p][:],
                    ALU.mult, ALU.add), chain=False)
                ep = ev(lambda v, b=bank2, p=dc % 2: v.tensor_add(
                    out_sb[p][:], out_sb[p][:], PS(b)))
                xT_free[dc % 2] = ep
                psum_free[bank] = (sv, ep)
                psum_free[bank2] = (sv, ep)
                ew("y", sv, ep)
                out_free[dc % 2] = dma(
                    "y", "out%d" % (dc % 2), lambda y, p=dc % 2,
                    r0=dc * 128, a=t0:
                    y.dma_start(outd[r0:r0 + 128, a:a + TS], out_sb[p][:]))
            fuse_ward = down_done[DC - 1]
            xn_t_free[t % 2] = up_done[IC - 1]
            if t + 1 < NT:
                ew("y", st_, wo_done[DC - 1])
                attn_t_val[t + 1] = dma(
                    "y", "att", lambda y, a=(t + 1) * TS:
                    y.dma_start(attn_t[:], ascr[:, :, a:a + TS]))

        ew("y", L["out0"], L["out0"].n)
        ew("y", L["out1"], L["out1"].n)

        with nc.Block() as block:
            @block.tensor
            def _(t):
                for f in streams["t"]:
                    f(t)

            @block.vector
            def _(v):
                for f in streams["v"]:
                    f(v)

            @block.scalar
            def _(s):
                for f in streams["s"]:
                    f(s)

            @block.gpsimd
            def _(g):
                for f in streams["g"]:
                    f(g)

            @block.sync
            def _(y):
                for f in streams["y"]:
                    f(y)
    return nc


def _host_layouts(W_q, W_k, W_v, W_o, wg, wu, wd, s_gate, s_up, s_down):
    perm = np.concatenate([np.arange(0, 128, 2), np.arange(1, 128, 2)])
    perm_q = np.concatenate([h * 128 + perm for h in range(H)])
    perm_k = np.concatenate([g * 128 + perm for g in range(G)])
    m = {}
    m["wq"] = np.ascontiguousarray(
        W_q[:, perm_q].reshape(DC, 128, H, 128).transpose(2, 1, 0, 3))
    m["wk"] = np.ascontiguousarray(
        W_k[:, perm_k].reshape(DC, 128, G, 128).transpose(2, 1, 0, 3))
    m["wv"] = np.ascontiguousarray(W_v.reshape(DC, 128, 512).transpose(1, 0, 2))
    m["wo"] = np.ascontiguousarray(
        W_o.reshape(DC, 128, DC, 128).transpose(2, 1, 0, 3))
    m["wg8"] = np.ascontiguousarray(
        wg.reshape(DC, 128, SBW, 256).transpose(2, 1, 0, 3))
    m["wu8"] = np.ascontiguousarray(
        wu.reshape(DC, 128, SBW, 256).transpose(2, 1, 0, 3))
    m["wd8"] = np.ascontiguousarray(
        wd.reshape(IC, 128, DC, 128).transpose(2, 1, 0, 3))
    m["sg"] = np.ascontiguousarray(s_gate.reshape(IC, 128).T)
    m["su"] = np.ascontiguousarray(s_up.reshape(IC, 128).T)
    m["sd"] = np.ascontiguousarray(s_down.reshape(DC, 128).T)
    half = 64
    freqs = (1.0 / 10000.0 ** (np.arange(half, dtype=np.float32) / half)
             ).astype(np.float32)
    ang = (np.arange(S)[:, None].astype(np.float32) * freqs[None, :]
           ).astype(np.float32)
    m["cosT"] = np.ascontiguousarray(np.cos(ang).astype(np.float32).T)
    m["sinT"] = np.ascontiguousarray(np.sin(ang).astype(np.float32).T)
    return m


def kernel(L0_attn_x, L0_attn_W_q, L0_attn_W_k, L0_attn_W_v, L0_attn_W_o,
           L0_ffn_w_gate, L0_ffn_w_up, L0_ffn_w_down,
           L0_ffn_s_gate, L0_ffn_s_up, L0_ffn_s_down):
    x = np.asarray(L0_attn_x, dtype=np.float32)
    shared = _host_layouts(
        np.asarray(L0_attn_W_q, np.float32), np.asarray(L0_attn_W_k, np.float32),
        np.asarray(L0_attn_W_v, np.float32), np.asarray(L0_attn_W_o, np.float32),
        np.asarray(L0_ffn_w_gate, np.int8), np.asarray(L0_ffn_w_up, np.int8),
        np.asarray(L0_ffn_w_down, np.int8),
        np.asarray(L0_ffn_s_gate, np.float32), np.asarray(L0_ffn_s_up, np.float32),
        np.asarray(L0_ffn_s_down, np.float32))
    nc = build_program()
    in_maps = []
    for c in range(8):
        m = dict(shared)
        m["x_nat"] = np.ascontiguousarray(x[c])
        m["xT"] = np.ascontiguousarray(x[c].T)
        in_maps.append(m)
    res = run_bass_kernel_spmd(nc, in_maps, core_ids=list(range(8)))
    out = np.stack([np.asarray(res.results[c]["outT"]).T for c in range(8)], axis=0)
    return np.ascontiguousarray(out.astype(np.float32))


# version 14
# speedup vs baseline: 1.0325x; 1.0325x over previous
import math
import contextlib
import numpy as np
import concourse.bass as bass
import concourse.mybir as mybir
from concourse.bass_utils import run_bass_kernel_spmd

F32 = mybir.dt.float32
BF16 = mybir.dt.bfloat16
I8 = mybir.dt.int8
AF = mybir.ActivationFunctionType
ALU = mybir.AluOpType
AX = mybir.AxisListType

S = 2048
D = 2048
H = 16
G = 4
DC = 16
NT = 4
IC = 64
SBW = 32
TS = 512
INV_SQRT_HD = 1.0 / math.sqrt(128.0)
NEG_CBOUND = -1.02 / (2.0 * math.sqrt(128.0))


class Sem:
    def __init__(self, h):
        self.h = h
        self.n = 0

    def inc(self, k=1):
        self.n += k
        return self.n


def build_program():
    nc = bass.Bass("TRN2", target_bir_lowering=False)
    dt = nc.dram_tensor
    x_nat = dt("x_nat", [S, D], F32, kind="ExternalInput").ap()
    xTd = dt("xT", [D, S], F32, kind="ExternalInput").ap()
    wqd = dt("wq", [H, 128, DC, 128], F32, kind="ExternalInput").ap()
    wkd = dt("wk", [G, 128, DC, 128], F32, kind="ExternalInput").ap()
    wvd = dt("wv", [128, DC, 512], F32, kind="ExternalInput").ap()
    wod = dt("wo", [DC, 128, DC, 128], F32, kind="ExternalInput").ap()
    wgd = dt("wg8", [SBW, 128, DC, 256], I8, kind="ExternalInput").ap()
    wud = dt("wu8", [SBW, 128, DC, 256], I8, kind="ExternalInput").ap()
    wdd = dt("wd8", [DC, 128, IC, 128], I8, kind="ExternalInput").ap()
    sgd = dt("sg", [128, IC], F32, kind="ExternalInput").ap()
    sud = dt("su", [128, IC], F32, kind="ExternalInput").ap()
    sdd = dt("sd", [128, DC], F32, kind="ExternalInput").ap()
    cosd = dt("cosT", [64, S], F32, kind="ExternalInput").ap()
    sind = dt("sinT", [64, S], F32, kind="ExternalInput").ap()
    outd = dt("outT", [D, S], F32, kind="ExternalOutput").ap()
    ascr = dt("attn_scr", [128, H, S], BF16).ap()
    xscr = dt("xn_scr", [128, DC, S], BF16).ap()

    streams = {"t": [], "v": [], "s": [], "g": [], "y": []}
    waited = {k: {} for k in streams}

    def em(eng, fn):
        streams[eng].append(fn)

    def ew(eng, sem, val):
        if val > 0 and waited[eng].get(id(sem.h), 0) < val:
            waited[eng][id(sem.h)] = val
            em(eng, lambda e, h=sem.h, v=val: e.wait_ge(h, v))

    stack = contextlib.ExitStack()
    with stack:
        mksem = lambda name: Sem(stack.enter_context(nc.semaphore(name)))
        st_ = mksem("s_t")
        sv = mksem("s_v")
        ss = mksem("s_s")
        sgc = mksem("s_gc")
        stq = [mksem("s_tq%d" % i) for i in range(NT)]
        L = {}
        for name in ("cs", "x0", "x1", "xscr", "wk", "wv", "wq0",
                     "wq1", "scl", "ao0", "ao1", "att", "xnt0", "xnt1",
                     "xt0", "xt1", "out0", "out1", "wg0", "wg1", "wu0",
                     "wu1", "wd0", "wd1", "wo0", "wo1"):
            L[name] = mksem("l_" + name)

        def dma(eng, lane, build):
            em(eng, lambda e, b=build, h=L[lane].h: b(e).then_inc(h, 16))
            return L[lane].inc(16)

        def ev(build, chain=True):
            prev = sv.n
            if chain:
                em("v", lambda v, b=build, p=prev: (
                    v.wait_ge(sv.h, p) if p else None,
                    b(v).then_inc(sv.h, 1)))
            else:
                em("v", lambda v, b=build: b(v).then_inc(sv.h, 1))
            return sv.inc(1)

        def es(build, chain=True):
            prev = ss.n
            if chain:
                em("s", lambda s, b=build, p=prev: (
                    s.wait_ge(ss.h, p) if p else None,
                    b(s).then_inc(ss.h, 1)))
            else:
                em("s", lambda s, b=build: b(s).then_inc(ss.h, 1))
            return ss.inc(1)

        ps_sc = stack.enter_context(nc.psum_tensor("ps_sc", [128, 4, 512], F32))
        ps_av = [stack.enter_context(nc.psum_tensor("ps_av%d" % i, [128, 512], F32))
                 for i in range(2)]
        ps_sm = [stack.enter_context(nc.psum_tensor("ps_sm%d" % i, [128, 512], F32))
                 for i in range(2)]

        def PS(b):
            if b < 4:
                return ps_sc[:, b, :]
            if b < 6:
                return ps_av[b - 4][:]
            return ps_sm[b - 6][:]

        psum_free = [None] * 8

        def te_claim(bank):
            c = psum_free[bank]
            if c is not None:
                ew("t", c[0], c[1])
                psum_free[bank] = None

        def mm_group(n, build_i):
            for i in range(n):
                if i < n - 1:
                    em("t", lambda t_, ii=i: build_i(t_, ii))
                else:
                    em("t", lambda t_, ii=i: build_i(t_, ii).then_inc(st_.h, 1))
            return st_.inc(1)

        sb_ = lambda name, shape, dtype: stack.enter_context(
            nc.sbuf_tensor(name, shape, dtype))

        ones_bf = sb_("ones_bf", [128, 128], BF16)
        ones1f = sb_("ones1f", [1, 128], F32)
        sg_sb = sb_("sg_sb", [128, IC], F32)
        su_sb = sb_("su_sb", [128, IC], F32)
        sd_sb = sb_("sd_sb", [128, DC], F32)
        stats = sb_("stats", [128, 2, 8], F32)
        smal = sb_("smal", [1, 32], F32)
        bias_h = sb_("bias_h", [128, 1], F32)
        rb_sb = sb_("rb_sb", [128, 512], F32)
        eps_sb = sb_("eps_sb", [128, 1], F32)
        isq_sb = sb_("isq_sb", [128, 1], F32)
        clo_sb = sb_("clo_sb", [128, 2], F32)

        import numpy as _np
        _eps_hi = float(_np.float32(1e-5))
        _eps_lo = float(_np.float32(1e-5 - _np.float64(_np.float32(1e-5))))
        _isq_hi = float(_np.float32(INV_SQRT_HD))
        _isq_lo = float(_np.float32(INV_SQRT_HD
                                    - _np.float64(_np.float32(INV_SQRT_HD))))
        em("g", lambda g: g.memset(ones_bf[:], 1.0))
        em("g", lambda g: g.memset(ones1f[:], 1.0))
        em("g", lambda g: g.memset(eps_sb[:], _eps_hi))
        em("g", lambda g: g.memset(isq_sb[:], _isq_hi))
        em("g", lambda g: g.memset(clo_sb[:, 0:1], _eps_lo))
        em("g", lambda g: g.memset(clo_sb[:, 1:2], _isq_lo).then_inc(sgc.h, 1))
        memset_done = sgc.inc(1)
        ew("v", sgc, memset_done)
        ev(lambda v: v.tensor_add(eps_sb[:], eps_sb[:], clo_sb[:, 0:1]))
        consts_v = ev(lambda v: v.tensor_add(
            isq_sb[:], isq_sb[:], clo_sb[:, 1:2]))
        ew("s", sv, consts_v)
        for dst, src in ((sg_sb, sgd), (su_sb, sud), (sd_sb, sdd)):
            dma("g", "scl", lambda g, d=dst, s=src: g.dma_start(d[:], s))
        scales_done = L["scl"].n

        with contextlib.ExitStack() as sa:
            sba = lambda name, shape, dtype: sa.enter_context(
                nc.sbuf_tensor(name, shape, dtype))
            xn = sba("xn", [128, DC, S], BF16)
            cos_sb = sba("cos_sb", [128, S], F32)
            sin_sb = sba("sin_sb", [128, S], F32)
            kr = sba("kr", [128, G, S], BF16)
            vtm = sba("vtm", [128, DC, 512], BF16)

            dma("y", "cs", lambda y: y.dma_start(cos_sb[0:64, :], cosd))
            dma("y", "cs", lambda y: y.dma_start(cos_sb[64:128, :], cosd))
            dma("y", "cs", lambda y: y.dma_start(sin_sb[0:64, :], sind))
            dma("y", "cs", lambda y: y.dma_start(sin_sb[64:128, :], sind))
            cos_done = L["cs"].n

            with contextlib.ExitStack() as s0:
                sb0 = lambda name, shape, dtype: s0.enter_context(
                    nc.sbuf_tensor(name, shape, dtype))
                x_t = [sb0("x_t%d" % i, [128, D], F32) for i in range(2)]
                xn_tm = sb0("xn_tm", [128, 16, D], BF16)
                scr = sb0("scr", [128, D], F32)

                applyv = [0] * 16
                for i in range(16):
                    par = i % 2
                    lane = "x%d" % par
                    if i == 0:
                        ew("s", sgc, memset_done)
                    if i >= 2:
                        ew("g", ss, applyv[i - 2])
                    xload = dma("g", lane, lambda g, p=par, i0=i * 128:
                                g.dma_start(x_t[p][:], x_nat[i0:i0 + 128, :]))
                    ew("s", L[lane], xload)
                    stats_done = es(lambda s, p=par: s.activation(
                        scr[:], x_t[p][:], AF.Square, accum_out=stats[:, p, 1:2]))
                    ew("v", L[lane], xload)
                    ev(lambda v, p=par: v.reduce_sum(
                        stats[:, p, 0:1], x_t[p][:], axis=AX.X))
                    ew("v", ss, stats_done)
                    ev(lambda v, p=par: v.tensor_scalar_mul(
                        stats[:, p, 2:3], stats[:, p, 0:1], 1.0 / D))
                    ev(lambda v, p=par: v.tensor_scalar_mul(
                        stats[:, p, 3:4], stats[:, p, 1:2], 1.0 / D))
                    ev(lambda v, p=par: v.tensor_mul(
                        stats[:, p, 4:5], stats[:, p, 2:3], stats[:, p, 2:3]))
                    varv = ev(lambda v, p=par: v.tensor_sub(
                        stats[:, p, 5:6], stats[:, p, 3:4], stats[:, p, 4:5]))
                    ew("s", sv, varv)
                    sqv = es(lambda s, p=par: s.activation(
                        stats[:, p, 6:7], stats[:, p, 5:6], AF.Sqrt,
                        bias=eps_sb[:]))
                    ew("v", ss, sqv)
                    ev(lambda v, p=par: v.reciprocal(
                        stats[:, p, 7:8], stats[:, p, 6:7]))
                    ev(lambda v, p=par: v.tensor_mul(
                        stats[:, p, 4:5], stats[:, p, 2:3], stats[:, p, 7:8]))
                    nbv = ev(lambda v, p=par: v.tensor_scalar_mul(
                        stats[:, p, 5:6], stats[:, p, 4:5], -1.0))
                    ew("s", sv, nbv)
                    applyv[i] = es(lambda s, p=par, ii=i: s.activation(
                        xn_tm[:, ii, :], x_t[p][:], AF.Identity,
                        bias=stats[:, p, 5:6], scale=stats[:, p, 7:8]))
                    ew("y", ss, applyv[i])
                    for hh in range(2):
                        em("y", lambda y, ii=i, h2=hh, qt=i // 4:
                           y.dma_start_transpose(
                               xn[:, 8 * h2:8 * h2 + 8,
                                  ii * 128:ii * 128 + 128],
                               xn_tm[:, ii, 1024 * h2:1024 * h2 + 1024])
                           .then_inc(stq[qt].h, 16))
                        stq[i // 4].inc(16)
                qdone = [stq[i].n for i in range(NT)]
                for i in range(NT):
                    ew("y", stq[i], qdone[i])
                xscr_done = dma("y", "xscr",
                                lambda y: y.dma_start(xscr, xn[:]))

            wq_sb = [sba("wq_sb%d" % i, [128, DC, 128], BF16) for i in range(2)]
            wk_sb = sba("wk_sb", [128, G, DC, 128], BF16)
            wv_sb = sba("wv_sb", [128, DC, 512], BF16)
            ew("g", ss, applyv[15])
            for i in range(NT):
                ew("g", stq[i], qdone[i])
            for gk in range(G):
                dma("g", "wk", lambda g, i=gk: g.dma_start(
                    wk_sb[:, i, :, :], wkd[i]))
            wk_done = L["wk"].n
            wv_done = dma("g", "wv", lambda g: g.dma_start(wv_sb[:], wvd))
            wq_val = [None] * H
            wq_val[0] = dma("g", "wq0",
                            lambda g: g.dma_start(wq_sb[0][:], wqd[0]))
            tmp0 = sba("tmp0", [64, 512], F32)
            tmp1 = sba("tmp1", [64, 512], F32)
            expT = [sba("expT%d" % i, [128, DC, 512], BF16) for i in range(2)]
            qr = sba("qr", [128, S], BF16)
            qps = sba("qps", [128, 4, 512], BF16)
            sqt = [sba("sqt%d" % i, [128, 512], BF16) for i in range(2)]
            attn_o = [sba("attn_o%d" % i, [128, 512], BF16) for i in range(2)]

            def rope(psum, dst_e, dst_o, t0):
                ev(lambda v, p=psum, a=t0: v.tensor_mul(
                    tmp0[:], p[0:64, :], cos_sb[0:64, a:a + TS]))
                ev(lambda v, p=psum, a=t0: v.tensor_mul(
                    tmp1[:], p[64:128, :], sin_sb[0:64, a:a + TS]))
                ev(lambda v, d=dst_e: v.tensor_sub(d, tmp0[:], tmp1[:]))
                ev(lambda v, p=psum, a=t0: v.tensor_mul(
                    tmp0[:], p[0:64, :], sin_sb[0:64, a:a + TS]))
                ev(lambda v, p=psum, a=t0: v.tensor_mul(
                    tmp1[:], p[64:128, :], cos_sb[0:64, a:a + TS]))
                return ev(lambda v, d=dst_o: v.tensor_add(d, tmp0[:], tmp1[:]))

            def rope_sb(src, dst_e, dst_o, t0):
                ev(lambda v, p=src, a=t0: v.tensor_mul(
                    tmp0[:], p[0:64, :], cos_sb[0:64, a:a + TS]))
                ev(lambda v, p=src, a=t0: v.tensor_mul(
                    tmp1[:], p[64:128, :], sin_sb[64:128, a:a + TS]))
                ev(lambda v, d=dst_e: v.tensor_sub(d, tmp0[:], tmp1[:]))
                ev(lambda v, p=src, a=t0: v.tensor_mul(
                    tmp0[:], p[0:64, :], sin_sb[0:64, a:a + TS]))
                ev(lambda v, p=src, a=t0: v.tensor_mul(
                    tmp1[:], p[64:128, :], cos_sb[64:128, a:a + TS]))
                return ev(lambda v, d=dst_o: v.tensor_add(d, tmp0[:], tmp1[:]))

            def norm_max(src_ap, acc_slot, first, wait_t):
                par = norm_max.par
                norm_max.par ^= 1
                ew("s", st_, wait_t)
                if norm_max.free[par]:
                    ew("s", st_, norm_max.free[par])
                sq_done = es(lambda s, p=par, a=src_ap: s.activation(
                    sqt[p][:], a, AF.Square), chain=False)
                te_claim(6)
                ew("t", ss, sq_done)
                nm = mm_group(1, lambda t_, i, p=par: t_.matmul(
                    ps_sm[0][0:1, :], ones_bf[:, 0:1], sqt[p][:],
                    start=True, stop=True))
                norm_max.free[par] = nm
                ew("v", st_, nm)
                if first:
                    rmx = ev(lambda v, sl=acc_slot: v.reduce_max(
                        smal[0:1, sl:sl + 1], ps_sm[0][0:1, :], axis=AX.X))
                else:
                    ev(lambda v: v.reduce_max(
                        smal[0:1, 26:27], ps_sm[0][0:1, :], axis=AX.X))
                    rmx = ev(lambda v, sl=acc_slot: v.tensor_max(
                        smal[0:1, sl:sl + 1], smal[0:1, sl:sl + 1],
                        smal[0:1, 26:27]))
                psum_free[6] = (sv, rmx)
                return sq_done
            norm_max.par = 0
            norm_max.free = [0, 0]

            ew("t", sgc, memset_done)
            ew("v", L["cs"], cos_done)
            ew("t", L["wk"], wk_done)
            ew("t", L["wv"], wv_done)
            krope = [[0] * NT for _ in range(G)]
            kproj_last = 0
            for t in range(NT):
                ew("t", stq[t], qdone[t])
                for gk in range(G):
                    bank = (t * G + gk) % 4
                    te_claim(bank)
                    kp = mm_group(DC, lambda t_, d, b=bank, gg=gk, a=t * TS:
                                  t_.matmul(PS(b), wk_sb[:, gg, d, :],
                                            xn[:, d, a:a + TS],
                                            start=(d == 0), stop=(d == DC - 1)))
                    kproj_last = kp
                    sqd = norm_max(PS(bank), gk, t == 0, kp)
                    ew("v", st_, kp)
                    ew("v", ss, sqd)
                    krope[gk][t] = rope(
                        PS(bank), kr[0:64, gk, t * TS:t * TS + TS],
                        kr[64:128, gk, t * TS:t * TS + TS], t * TS)
                    psum_free[bank] = (sv, krope[gk][t])
                for tc in range(t * 4, t * 4 + 4):
                    bank = 4 + tc % 2
                    te_claim(bank)
                    vp = mm_group(DC, lambda t_, d, b=bank, c0=tc * 128:
                                  t_.matmul(PS(b), xn[:, d, c0:c0 + 128],
                                            wv_sb[:, d, :],
                                            start=(d == 0), stop=(d == DC - 1)))
                    ew("v", st_, vp)
                    vev = ev(lambda v, b=bank, c=tc: v.tensor_copy(
                        vtm[:, c, :], PS(b)), chain=False)
                    psum_free[bank] = (sv, vev)

            qproj_done = [0] * H
            expT_free = [0, 0]
            attn_o_free = [0, 0]
            exp_last = 0
            pend = None
            av_done = {}
            sums_done = {}

            def emit_avsums_batch(p, b):
                gkp, qq, pq = p
                if b == 0:
                    te_claim(4 + pq)
                    te_claim(6 + pq)
                    ew("t", ss, exp_last_of[(gkp, qq, pq)])
                for j in range(4):
                    kc = 4 * b + j
                    last = kc == DC - 1
                    em("t", lambda t_, k=kc, gg=gkp, p2=pq, la=last:
                       (t_.matmul(ps_av[p2][:], vtm[:, k, gg * 128:gg * 128 + 128],
                                  expT[p2][:, k, :], start=(k == 0),
                                  stop=(k == DC - 1), skip_group_check=True)
                        .then_inc(st_.h, 1)) if la else
                       t_.matmul(ps_av[p2][:], vtm[:, k, gg * 128:gg * 128 + 128],
                                 expT[p2][:, k, :], start=(k == 0),
                                 stop=(k == DC - 1), skip_group_check=True))
                    if last:
                        av_done[p] = st_.inc(1)
                for j in range(4):
                    kc = 4 * b + j
                    last = kc == DC - 1
                    em("t", lambda t_, k=kc, p2=pq, la=last:
                       (t_.matmul(ps_sm[p2][:], ones_bf[:], expT[p2][:, k, :],
                                  start=(k == 0), stop=(k == DC - 1),
                                  skip_group_check=True).then_inc(st_.h, 1))
                       if la else
                       t_.matmul(ps_sm[p2][:], ones_bf[:], expT[p2][:, k, :],
                                 start=(k == 0), stop=(k == DC - 1),
                                 skip_group_check=True))
                    if last:
                        sums_done[p] = st_.inc(1)

            def emit_attn_epilogue(p, hh, qq):
                gkp, q2, pq = p
                ew("v", st_, sums_done[p])
                if attn_o_free[pq]:
                    ew("v", L["ao%d" % pq], attn_o_free[pq])
                ev(lambda v, p2=pq: v.reciprocal(rb_sb[:], ps_sm[p2][:]))
                ao = ev(lambda v, p2=pq: v.tensor_mul(
                    attn_o[p2][:], ps_av[p2][:], rb_sb[:]))
                psum_free[4 + pq] = (sv, ao)
                psum_free[6 + pq] = (sv, ao)
                ew("y", sv, ao)
                attn_o_free[pq] = dma(
                    "y", "ao%d" % pq, lambda y, h3=hh, p2=pq, a=qq * TS:
                    y.dma_start(ascr[:, h3, a:a + TS], attn_o[p2][:]))
                expT_free[pq] = sums_done[p]

            exp_last_of = {}
            pend_head = [None, None]
            for h in range(H):
                gk = h % 4
                if h + 1 < H:
                    if h >= 1:
                        ew("g", st_, qproj_done[h - 1])
                    wq_val[h + 1] = dma(
                        "g", "wq%d" % ((h + 1) % 2), lambda g, i=h + 1,
                        p=(h + 1) % 2: g.dma_start(wq_sb[p][:], wqd[i]))
                ew("t", L["wq%d" % (h % 2)], wq_val[h])
                qrope = [0] * NT
                for t in range(NT):
                    te_claim(t)
                    qp = mm_group(DC, lambda t_, d, tt=t, hh=h:
                                  t_.matmul(ps_sc[:, tt, :],
                                            wq_sb[hh % 2][:, d, :],
                                            xn[:, d, tt * TS:tt * TS + TS],
                                            start=(d == 0), stop=(d == DC - 1)))
                    qproj_done[h] = qp
                    norm_max(ps_sc[:, t, :], 16, t == 0, qp)
                    evv = es(lambda s, tt=t: s.copy(
                        qps[:, tt, :], ps_sc[:, tt, :]), chain=False)
                    psum_free[t] = (ss, evv)
                qps_done = ss.n
                ev(lambda v, gg=gk: v.tensor_add(
                    smal[0:1, 24:25], smal[0:1, 16:17], smal[0:1, gg:gg + 1]))
                negc = ev(lambda v: v.tensor_scalar_mul(
                    smal[0:1, 25:26], smal[0:1, 24:25], NEG_CBOUND))
                te_claim(6)
                ew("t", sv, negc)
                bmm = mm_group(1, lambda t_, i: t_.matmul(
                    ps_sm[0][:, 0:1], ones1f[:], smal[0:1, 25:26],
                    start=True, stop=True))
                if exp_last:
                    ew("v", ss, exp_last)
                ew("v", st_, bmm)
                bias_done = ev(lambda v: v.tensor_copy(
                    bias_h[:], ps_sm[0][:, 0:1]))
                psum_free[6] = (sv, bias_done)
                ew("v", ss, qps_done)
                for t in range(NT):
                    qrope[t] = rope_sb(qps[:, t, :],
                                       qr[0:64, t * TS:t * TS + TS],
                                       qr[64:128, t * TS:t * TS + TS], t * TS)
                for q in range(NT):
                    qp = q % 2
                    exp_b = [0] * 4
                    for b in range(4):
                        if pend is not None:
                            emit_avsums_batch(pend, b)
                        if b == 0:
                            ew("t", sv, qrope[q])
                        for j in range(4):
                            te_claim(j)
                        for j in range(4):
                            kc = 4 * b + j
                            last = j == 3
                            em("t", lambda t_, jj=j, k=kc, gg=gk, a=q * TS,
                               la=last:
                               (t_.matmul(ps_sc[:, jj, :],
                                          kr[:, gg, k * 128:k * 128 + 128],
                                          qr[:, a:a + TS], start=True,
                                          stop=True, skip_group_check=True)
                                .then_inc(st_.h, 1)) if la else
                               t_.matmul(ps_sc[:, jj, :],
                                         kr[:, gg, k * 128:k * 128 + 128],
                                         qr[:, a:a + TS], start=True,
                                         stop=True, skip_group_check=True))
                        sc_done = st_.inc(1)
                        if q == 0 and b == 0:
                            ew("s", sv, bias_done)
                        if b == 0 and expT_free[qp]:
                            ew("s", st_, expT_free[qp])
                        ew("s", st_, sc_done)
                        exp_b[b] = es(lambda s, p=qp, b0=4 * b: s.activation(
                            expT[p][:, b0:b0 + 4, :], ps_sc[:, :, :], AF.Exp,
                            bias=bias_h[:], scale=isq_sb[:]), chain=False)
                        for j in range(4):
                            psum_free[j] = (ss, exp_b[b])
                    exp_last = exp_b[3]
                    exp_last_of[(gk, q, qp)] = exp_last
                    if pend is not None:
                        emit_attn_epilogue(pend, pend_head[0], pend_head[1])
                    pend = (gk, q, qp)
                    pend_head = [h, q]
            for b in range(4):
                emit_avsums_batch(pend, b)
            emit_attn_epilogue(pend, pend_head[0], pend_head[1])
            ao_tot = [L["ao0"].n, L["ao1"].n]
            pA_t, pA_v, pA_s = st_.n, sv.n, ss.n

        sbb = lambda name, shape, dtype: stack.enter_context(
            nc.sbuf_tensor(name, shape, dtype))
        fuse = sbb("fuse", [128, IC, TS], BF16)
        xn_t = [sbb("xn_t%d" % i, [128, DC, TS], BF16) for i in range(2)]
        wg_sb = [sbb("wg_sb%d" % i, [128, DC, 256], BF16) for i in range(2)]
        wu_sb = [sbb("wu_sb%d" % i, [128, DC, 256], BF16) for i in range(2)]
        wd_sb = [sbb("wd_sb%d" % i, [128, IC, 128], BF16) for i in range(2)]
        wo_sb = [sbb("wo_sb%d" % i, [128, DC, 128], BF16) for i in range(2)]
        attn_t = sbb("attn_t", [128, H, TS], BF16)
        gate_sb = [sbb("gate_sb%d" % i, [128, TS], BF16) for i in range(2)]
        xT_sb = [sbb("xT_sb%d" % i, [128, TS], F32) for i in range(2)]
        out_sb = [sbb("out_sb%d" % i, [128, TS], F32) for i in range(2)]

        ew("g", st_, pA_t)
        ew("g", sv, pA_v)
        ew("g", ss, pA_s)
        ew("g", L["ao0"], ao_tot[0])
        ew("g", L["ao1"], ao_tot[1])
        ew("y", st_, pA_t)
        ew("y", sv, pA_v)
        ew("y", L["ao0"], ao_tot[0])
        ew("y", L["ao1"], ao_tot[1])

        wg_val = {}
        wu_val = {}
        wgu_buf_free = [0, 0]
        wd_val = {}
        wo_val = {}
        wd_buf_free = [0, 0]
        wo_buf_free = [0, 0]
        xn_t_val = [0] * NT
        xn_t_free = [0, 0]
        attn_t_val = [0] * NT
        gate_free = [0, 0]
        fuse_ward = 0
        out_free = [0, 0]
        xT_free = [0, 0]

        def load_wgu(t, sb):
            if wgu_buf_free[sb % 2]:
                ew("g", st_, wgu_buf_free[sb % 2])
                wgu_buf_free[sb % 2] = 0
            wg_val[(t, sb)] = dma(
                "g", "wg%d" % (sb % 2), lambda g, p=sb % 2, s=sb:
                g.dma_start(wg_sb[p][:], wgd[s]))
            wu_val[(t, sb)] = dma(
                "g", "wu%d" % (sb % 2), lambda g, p=sb % 2, s=sb:
                g.dma_start(wu_sb[p][:], wud[s]))

        def load_wdo(t, dc):
            if wd_buf_free[dc % 2]:
                ew("g", st_, wd_buf_free[dc % 2])
                wd_buf_free[dc % 2] = 0
            wd_val[(t, dc)] = dma(
                "g", "wd%d" % (dc % 2), lambda g, p=dc % 2, d=dc:
                g.dma_start(wd_sb[p][:], wdd[d]))
            if wo_buf_free[dc % 2]:
                ew("g", st_, wo_buf_free[dc % 2])
                wo_buf_free[dc % 2] = 0
            wo_val[(t, dc)] = dma(
                "g", "wo%d" % (dc % 2), lambda g, p=dc % 2, d=dc:
                g.dma_start(wo_sb[p][:], wod[d]))

        ew("y", L["xscr"], xscr_done)
        xn_t_val[0] = dma("y", "xnt0", lambda y: y.dma_start(
            xn_t[0][:], xscr[:, :, 0:TS]))
        attn_t_val[0] = dma("y", "att", lambda y: y.dma_start(
            attn_t[:], ascr[:, :, 0:TS]))
        load_wgu(0, 0)

        for t in range(NT):
            t0 = t * TS
            up_done = [0] * IC
            fuse_done = [0] * IC
            down_done = [0] * DC
            wo_done = [0] * DC
            for ic in range(IC):
                sb = ic // 2
                jc = (ic % 2) * 128
                if ic % 2 == 0:
                    if sb + 1 < SBW:
                        load_wgu(t, sb + 1)
                    elif t + 1 < NT:
                        load_wgu(t + 1, 0)
                if ic == 4:
                    load_wdo(t, 0)
                if ic == 8:
                    load_wdo(t, 1)
                bank = ic % 2
                te_claim(bank)
                ew("t", L["wg%d" % (sb % 2)], wg_val[(t, sb)])
                if ic == 0:
                    ew("t", L["xnt%d" % (t % 2)], xn_t_val[t])
                gate_done = mm_group(
                    DC, lambda t_, d, b=bank, p=sb % 2, j=jc, tp=t % 2:
                    t_.matmul(PS(b), wg_sb[p][:, d, j:j + 128],
                              xn_t[tp][:, d, :],
                              start=(d == 0), stop=(d == DC - 1)))
                ew("s", st_, gate_done)
                if t == 0 and ic == 0:
                    ew("s", L["scl"], scales_done)
                if gate_free[ic % 2]:
                    ew("s", sv, gate_free[ic % 2])
                silu_v = es(lambda s, b=bank, p=ic % 2, i=ic: s.activation(
                    gate_sb[p][:], PS(b), AF.Silu, scale=sg_sb[:, i:i + 1]),
                    chain=False)
                psum_free[bank] = (ss, silu_v)
                bank2 = 2 + ic % 2
                te_claim(bank2)
                ew("t", L["wu%d" % (sb % 2)], wu_val[(t, sb)])
                up_done[ic] = mm_group(
                    DC, lambda t_, d, b=bank2, p=sb % 2, j=jc, tp=t % 2:
                    t_.matmul(PS(b), wu_sb[p][:, d, j:j + 128],
                              xn_t[tp][:, d, :],
                              start=(d == 0), stop=(d == DC - 1)))
                wgu_buf_free[sb % 2] = up_done[ic]
                ew("v", st_, up_done[ic])
                ew("v", ss, silu_v)
                if ic == 0 and fuse_ward:
                    ew("v", st_, fuse_ward)
                fuse_done[ic] = ev(
                    lambda v, b=bank2, p=ic % 2, i=ic: v.scalar_tensor_tensor(
                        fuse[:, i, :], PS(b), su_sb[:, i:i + 1], gate_sb[p][:],
                        ALU.mult, ALU.mult), chain=False)
                gate_free[ic % 2] = fuse_done[ic]
                psum_free[bank2] = (sv, fuse_done[ic])
            if t + 1 < NT:
                if xn_t_free[(t + 1) % 2]:
                    ew("y", st_, xn_t_free[(t + 1) % 2])
                xn_t_val[t + 1] = dma(
                    "y", "xnt%d" % ((t + 1) % 2),
                    lambda y, a=(t + 1) * TS, p=(t + 1) % 2:
                    y.dma_start(xn_t[p][:], xscr[:, :, a:a + TS]))
            for dc in range(DC):
                bank = 4 + dc % 2
                te_claim(bank)
                ew("t", L["wd%d" % (dc % 2)], wd_val[(t, dc)])
                if dc == 0:
                    ew("t", sv, fuse_done[IC - 1])
                down_done[dc] = mm_group(
                    IC, lambda t_, i, b=bank, p=dc % 2:
                    t_.matmul(PS(b), wd_sb[p][:, i, :], fuse[:, i, :],
                              start=(i == 0), stop=(i == IC - 1)))
                wd_buf_free[dc % 2] = down_done[dc]
                bank2 = 6 + dc % 2
                te_claim(bank2)
                ew("t", L["wo%d" % (dc % 2)], wo_val[(t, dc)])
                if dc == 0:
                    ew("t", L["att"], attn_t_val[t])
                wo_done[dc] = mm_group(
                    DC, lambda t_, f, b=bank2, p=dc % 2:
                    t_.matmul(PS(b), wo_sb[p][:, f, :], attn_t[:, f, :],
                              start=(f == 0), stop=(f == DC - 1)))
                wo_buf_free[dc % 2] = wo_done[dc]
                if 1 <= dc < DC - 1:
                    load_wdo(t, dc + 1)
                xtlane = "xt%d" % (dc % 2)
                if xT_free[dc % 2]:
                    ew("y", sv, xT_free[dc % 2])
                xT_val = dma("y", xtlane, lambda y, p=dc % 2, r0=dc * 128, a=t0:
                             y.dma_start(xT_sb[p][:], xTd[r0:r0 + 128, a:a + TS]))
                ew("v", st_, wo_done[dc])
                ew("v", L[xtlane], xT_val)
                if out_free[dc % 2]:
                    ew("v", L["out%d" % (dc % 2)], out_free[dc % 2])
                ev(lambda v, b=bank, p=dc % 2, d=dc: v.scalar_tensor_tensor(
                    out_sb[p][:], PS(b), sd_sb[:, d:d + 1], xT_sb[p][:],
                    ALU.mult, ALU.add), chain=False)
                ep = ev(lambda v, b=bank2, p=dc % 2: v.tensor_add(
                    out_sb[p][:], out_sb[p][:], PS(b)))
                xT_free[dc % 2] = ep
                psum_free[bank] = (sv, ep)
                psum_free[bank2] = (sv, ep)
                ew("y", sv, ep)
                out_free[dc % 2] = dma(
                    "y", "out%d" % (dc % 2), lambda y, p=dc % 2,
                    r0=dc * 128, a=t0:
                    y.dma_start(outd[r0:r0 + 128, a:a + TS], out_sb[p][:]))
            fuse_ward = down_done[DC - 1]
            xn_t_free[t % 2] = up_done[IC - 1]
            if t + 1 < NT:
                ew("y", st_, wo_done[DC - 1])
                attn_t_val[t + 1] = dma(
                    "y", "att", lambda y, a=(t + 1) * TS:
                    y.dma_start(attn_t[:], ascr[:, :, a:a + TS]))

        ew("y", L["out0"], L["out0"].n)
        ew("y", L["out1"], L["out1"].n)

        with nc.Block() as block:
            @block.tensor
            def _(t):
                for f in streams["t"]:
                    f(t)

            @block.vector
            def _(v):
                for f in streams["v"]:
                    f(v)

            @block.scalar
            def _(s):
                for f in streams["s"]:
                    f(s)

            @block.gpsimd
            def _(g):
                for f in streams["g"]:
                    f(g)

            @block.sync
            def _(y):
                for f in streams["y"]:
                    f(y)
    return nc


def _host_layouts(W_q, W_k, W_v, W_o, wg, wu, wd, s_gate, s_up, s_down):
    perm = np.concatenate([np.arange(0, 128, 2), np.arange(1, 128, 2)])
    perm_q = np.concatenate([h * 128 + perm for h in range(H)])
    perm_k = np.concatenate([g * 128 + perm for g in range(G)])
    m = {}
    m["wq"] = np.ascontiguousarray(
        W_q[:, perm_q].reshape(DC, 128, H, 128).transpose(2, 1, 0, 3))
    m["wk"] = np.ascontiguousarray(
        W_k[:, perm_k].reshape(DC, 128, G, 128).transpose(2, 1, 0, 3))
    m["wv"] = np.ascontiguousarray(W_v.reshape(DC, 128, 512).transpose(1, 0, 2))
    m["wo"] = np.ascontiguousarray(
        W_o.reshape(DC, 128, DC, 128).transpose(2, 1, 0, 3))
    m["wg8"] = np.ascontiguousarray(
        wg.reshape(DC, 128, SBW, 256).transpose(2, 1, 0, 3))
    m["wu8"] = np.ascontiguousarray(
        wu.reshape(DC, 128, SBW, 256).transpose(2, 1, 0, 3))
    m["wd8"] = np.ascontiguousarray(
        wd.reshape(IC, 128, DC, 128).transpose(2, 1, 0, 3))
    m["sg"] = np.ascontiguousarray(s_gate.reshape(IC, 128).T)
    m["su"] = np.ascontiguousarray(s_up.reshape(IC, 128).T)
    m["sd"] = np.ascontiguousarray(s_down.reshape(DC, 128).T)
    half = 64
    freqs = (1.0 / 10000.0 ** (np.arange(half) / np.float64(half))
             ).astype(np.float32)
    ang = (np.arange(S)[:, None].astype(np.float32) * freqs[None, :]
           ).astype(np.float32)
    m["cosT"] = np.ascontiguousarray(np.cos(ang).astype(np.float32).T)
    m["sinT"] = np.ascontiguousarray(np.sin(ang).astype(np.float32).T)
    return m


def kernel(L0_attn_x, L0_attn_W_q, L0_attn_W_k, L0_attn_W_v, L0_attn_W_o,
           L0_ffn_w_gate, L0_ffn_w_up, L0_ffn_w_down,
           L0_ffn_s_gate, L0_ffn_s_up, L0_ffn_s_down):
    x = np.asarray(L0_attn_x, dtype=np.float32)
    shared = _host_layouts(
        np.asarray(L0_attn_W_q, np.float32), np.asarray(L0_attn_W_k, np.float32),
        np.asarray(L0_attn_W_v, np.float32), np.asarray(L0_attn_W_o, np.float32),
        np.asarray(L0_ffn_w_gate, np.int8), np.asarray(L0_ffn_w_up, np.int8),
        np.asarray(L0_ffn_w_down, np.int8),
        np.asarray(L0_ffn_s_gate, np.float32), np.asarray(L0_ffn_s_up, np.float32),
        np.asarray(L0_ffn_s_down, np.float32))
    nc = build_program()
    in_maps = []
    for c in range(8):
        m = dict(shared)
        m["x_nat"] = np.ascontiguousarray(x[c])
        m["xT"] = np.ascontiguousarray(x[c].T)
        in_maps.append(m)
    res = run_bass_kernel_spmd(nc, in_maps, core_ids=list(range(8)))
    out = np.stack([np.asarray(res.results[c]["outT"]).T for c in range(8)], axis=0)
    return np.ascontiguousarray(out.astype(np.float32))
